# version 14
# baseline (speedup 1.0000x reference)
import numpy as np
from contextlib import ExitStack

import concourse.bass as bass
import concourse.mybir as mybir
from concourse.bass_utils import run_bass_kernel_spmd

F32 = mybir.dt.float32
BF16 = mybir.dt.bfloat16
F16 = mybir.dt.float16
I32 = mybir.dt.int32
AF = mybir.ActivationFunctionType
ALU = mybir.AluOpType
AXX = mybir.AxisListType.X

S = 2048
DM = 2048
H = 16
FI = 5632
NT = S // 128
ND = DM // 128
GW = DM // 8
DWH = (FI // 8) // 2
NJ = FI // 128

ENGS = ("t", "v", "s", "y", "g")
import os
DBG_BAR = int(os.environ.get("KDBG_BAR", "0"))
DBG_SEQPREP = int(os.environ.get("KDBG_SEQPREP", "0"))
KSTOP = os.environ.get("KSTOP", "")


class StopEmission(Exception):
    pass


class KB:
    def __init__(self, nc):
        self.nc = nc
        self.ops = {e: [] for e in ENGS}
        self.cnt = {}
        self.semh = {}
        for name in ("pe", "ve", "ac", "dg", "ds", "so"):
            self.cnt[name] = 0
            self.semh[name] = nc.alloc_semaphore(name)
        self.dirty = {}
        self.flushops = {}

    def w(self, st, sem, val=None):
        v = self.cnt[sem] if val is None else val
        if v <= 0:
            return
        h = self.semh[sem]
        self.ops[st].append(lambda e, h=h, v=v: e.wait_ge(h, v))

    def set_flushop(self, sem, st, fn):
        self.flushops[sem] = (st, fn)
        self.dirty[sem] = False

    def flush(self, sem):
        if not self.dirty.get(sem, False):
            return self.cnt[sem]
        st, fn = self.flushops[sem]
        self.dirty[sem] = False
        return self.do(st, fn, inc=sem)

    def wf(self, st, sem, val=None):
        if sem in self.dirty:
            self.flush(sem)
        self.w(st, sem, val)

    def do(self, st, fn, inc=None, n=1):
        if inc is None:
            self.ops[st].append(lambda e, f=fn: f(e))
            return None
        if inc in self.dirty:
            self.dirty[inc] = True
        self.cnt[inc] += n
        h = self.semh[inc]
        self.ops[st].append(lambda e, f=fn, h=h, m=n: f(e).then_inc(h, m))
        return self.cnt[inc]

    def dma(self, st, out, in_, inc="ds"):
        return self.do(st, lambda e, o=out, i=in_: e.dma_start(out=o, in_=i), inc=inc, n=16)

    def dmat(self, st, out, in_, inc="ds"):
        return self.do(st, lambda e, o=out, i=in_: e.dma_start_transpose(out=o, in_=i),
                       inc=inc, n=16)

    def barrier(self):
        for sem in list(self.dirty):
            self.flush(sem)
        snap = dict(self.cnt)
        for st in ENGS:
            for sem in ("pe", "ve", "ac", "dg", "ds"):
                self.w(st, sem, snap[sem])


def build_core_program():
    nc = bass.Bass("TRN2", target_bir_lowering=False, detect_race_conditions=False)
    kb = KB(nc)

    x_h = nc.dram_tensor("x", [S, DM], F32, kind="ExternalInput").ap()
    wqkv_h = nc.dram_tensor("wqkv", [3 * DM, DM], F32, kind="ExternalInput").ap()
    bqkv_h = nc.dram_tensor("bqkv", [3 * DM], F32, kind="ExternalInput").ap()
    wo_h = nc.dram_tensor("wo", [DM, DM], F32, kind="ExternalInput").ap()
    bo_h = nc.dram_tensor("bo", [DM], F32, kind="ExternalInput").ap()
    gup_h = nc.dram_tensor("gup", [2 * FI + 8, GW], I32, kind="ExternalInput").ap()
    gus_h = nc.dram_tensor("gus", [16, 2 * FI], F16, kind="ExternalInput").ap()
    dnp_h = nc.dram_tensor("dnp", [DM, 2 * DWH], I32, kind="ExternalInput").ap()
    dns_h = nc.dram_tensor("dns", [44, DM], F16, kind="ExternalInput").ap()
    idb_h = nc.dram_tensor("idb", [128, 128], BF16, kind="ExternalInput").ap()
    idh_h = nc.dram_tensor("idh", [128, 128], F16, kind="ExternalInput").ap()
    idf_h = nc.dram_tensor("idf", [128, 128], F32, kind="ExternalInput").ap()
    out_h = nc.dram_tensor("out", [S, DM], F32, kind="ExternalOutput").ap()

    wgu_nat = nc.dram_tensor("wgu_nat", [2 * FI, DM], BF16, kind="Internal").ap()
    wdn_nat = nc.dram_tensor("wdn_nat", [DM, FI], BF16, kind="Internal").ap()
    wgu_t = nc.dram_tensor("wgu_t", [ND, 2 * NJ, 128, 128], BF16, kind="Internal").ap()
    wdn_t = nc.dram_tensor("wdn_t", [NJ, 4, 128, 512], BF16, kind="Internal").ap()
    ctxT_hbm = nc.dram_tensor("ctxT", [H, 128, S], BF16, kind="Internal").ap()
    x1_hbm = nc.dram_tensor("x1", [S, DM], BF16, kind="Internal").ap()

    pctx = ExitStack()
    psb = lambda name, shape, dt: pctx.enter_context(nc.sbuf_tensor(name, shape, dt))
    xnT = psb("xnT", [128, ND * S], BF16)
    xn_tm = psb("xn_tm", [128, DM + 8], BF16)
    dump = psb("dump", [128, DM], F32)
    cvec = psb("cvec", [128, 16], F32)
    svec = psb("svec", [128, 16], F32)
    scl_g = psb("scl_g", [128, 2 * NJ * 16], F32)
    scl_d = psb("scl_d", [128, 16 * 44], F32)
    qkvb_fm = psb("qkvb_fm", [128, 48], F32)
    idb = psb("idb_s", [128, 128], BF16)
    idh = psb("idh_s", [128, 128], F16)
    idf = psb("idf_s", [128, 128], F32)
    PS = pctx.enter_context(nc.psum_tensor("PS", [128, 4096], F32))
    PSa = PS[:]
    PSB = PSa.bitcast(BF16)
    PSH = PSa.bitcast(F16)

    def bank32(b, n=512):
        return PSa[:, 512 * b: 512 * b + n]

    def bankbf(b, n=1024):
        return PSB[:, 1024 * b: 1024 * b + n]

    C126, CSM, CSH, CN, CE5, CE11, CM1 = range(7)
    cc = lambda i: cvec[:, i:i + 1]
    sv = lambda i: svec[:, i:i + 1]
    svb = lambda i, n: svec[:, i:i + 1].broadcast_to([128, n])

    bank_free = [("pe", 0)] * 8

    def acquire(st, b):
        sem, v = bank_free[b]
        kb.w(st, sem, v)

    def release(b, sem):
        bank_free[b] = (sem, kb.cnt[sem])

    def sp():
        kb.do("v", lambda e: e.tensor_copy(out=sv(13), in_=cc(6)))

    def maybe_stop(tag, dumps):
        if KSTOP != tag:
            return
        kb.barrier()
        for i, ap in enumerate(dumps):
            dbg = nc.dram_tensor("kdbg_%d" % i, list(ap.shape), ap.dtype,
                                 kind="ExternalOutput").ap()
            kb.dma("g", dbg, ap, inc="dg")
        raise StopEmission

    def ln_chain(eps_col, extra_eps_col=None):
        kb.do("v", lambda e: e.tensor_scalar(out=sv(2), in0=sv(0), scalar1=cc(CN),
                                             scalar2=None, op0=ALU.mult))
        kb.do("v", lambda e: e.tensor_scalar(out=sv(3), in0=sv(1), scalar1=cc(CN),
                                             scalar2=None, op0=ALU.mult))
        kb.do("v", lambda e: e.tensor_tensor(out=sv(4), in0=sv(2), in1=sv(2), op=ALU.mult))
        sp()
        kb.do("v", lambda e: e.tensor_tensor(out=sv(5), in0=sv(3), in1=sv(4),
                                             op=ALU.subtract))
        sp()
        if extra_eps_col is None:
            kb.do("v", lambda e: e.tensor_scalar(out=sv(6), in0=sv(5),
                                                 scalar1=cc(eps_col), scalar2=None,
                                                 op0=ALU.add), inc="ve")
        else:
            kb.do("v", lambda e: e.scalar_tensor_tensor(out=sv(6), in0=sv(3),
                                                        scalar=cc(eps_col), in1=sv(5),
                                                        op0=ALU.mult, op1=ALU.add))
            sp()
            kb.do("v", lambda e: e.tensor_scalar(out=sv(6), in0=sv(6),
                                                 scalar1=cc(extra_eps_col), scalar2=None,
                                                 op0=ALU.add), inc="ve")
        kb.wf("s", "ve")
        kb.do("s", lambda e: e.activation(out=sv(7), in_=sv(6), func=AF.Sqrt), inc="ac")
        kb.wf("v", "ac")
        kb.do("v", lambda e: e.reciprocal(out=sv(8), in_=sv(7)))
        sp()
        kb.do("v", lambda e: e.scalar_tensor_tensor(out=sv(9), in0=sv(2), scalar=cc(CM1),
                                                    in1=sv(8), op0=ALU.mult, op1=ALU.mult))
        sp()
        sp()

    def ln_stats(xbuf):
        kb.do("v", lambda e, x=xbuf: e.reduce_sum(sv(0), x, AXX), inc="ve")
        kb.wf("s", "ve")
        kb.do("s", lambda e, x=xbuf: e.activation(out=dump[:], in_=x, func=AF.Square,
                                                  accum_out=sv(1)), inc="ac")
        kb.wf("v", "ac")

    def comb_ap(base_ap, base):
        v = base_ap[:, base:base + 512]
        v = v.rearrange("p (c e) -> p c e", e=8)
        return v[:, :, 0:8:4]

    def xnT_rhs(dc, lo, n):
        return xnT[:, dc * S + lo: dc * S + lo + n]

    def emit_xn_transposes(tc, permuted):
        kb.wf("t", "ve")
        for halfb in range(2):
            b = halfb
            acquire("t", b)
            for i in range(8):
                dc = halfb * 8 + i
                if permuted:
                    kk, lc = dc // 4, dc % 4
                    src = comb_ap(xn_tm[:], 512 * lc + kk)
                else:
                    src = xn_tm[:, dc * 128:(dc + 1) * 128]
                kb.do("t", lambda e, d=PSB[:, 1024 * b + i * 128: 1024 * b + (i + 1) * 128],
                      s=src: e.transpose(d, s, idb[:]),
                      inc=("pe" if i == 7 else None))
            kb.w("v", "pe")
            dst = xnT[:].rearrange("p (d t) -> p d t", t=S)[:, halfb * 8: halfb * 8 + 8,
                                                           tc * 128:(tc + 1) * 128]
            kb.do("v", lambda e, d=dst, s=bankbf(b): e.tensor_copy(out=d, in_=s), inc="ve")
            release(b, "ve")

    try:
        ctx0 = ExitStack()
        sgu = ctx0.enter_context(nc.sbuf_tensor("sgu", [16, 2 * FI + 8], F16))
        sdn = ctx0.enter_context(nc.sbuf_tensor("sdn", [44, DM], F16))
        qkvb_n = ctx0.enter_context(nc.sbuf_tensor("qkvb_n", [48, 128], F32))

        kb.dma("y", idb[:], idb_h)
        kb.dma("y", idh[:], idh_h)
        kb.dma("y", idf[:], idf_h)
        kb.dma("g", sgu[:, 0:2 * FI], gus_h, inc="dg")
        kb.dma("g", sdn[:], dns_h, inc="dg")
        kb.dma("g", qkvb_n[:], bqkv_h.rearrange("(a b) -> a b", b=128), inc="dg")

        for col, val in ((C126, float(2.0 ** 126)),
                         (CSM, float(1.0 / np.float32(np.sqrt(np.float32(128.0))))),
                         (CSH, -12.0), (CN, 1.0 / 2048.0), (CE5, 1e-5), (CE11, 1e-11),
                         (CM1, -1.0)):
            kb.do("v", lambda e, c=col, v=val: e.memset(cvec[:, c:c + 1], v), inc="ve")

        kb.set_flushop("ve", "v", lambda e: e.tensor_copy(out=sv(15), in_=cc(6)))
        kb.set_flushop("ac", "s", lambda e: e.copy(out=sv(14), in_=cc(6)))
        kb.w("t", "ds")
        kb.w("t", "dg")
        kb.do("t", lambda e: e.transpose(bank32(2, 48), qkvb_n[:], idf[0:48, 0:48]), inc="pe")
        kb.w("v", "pe")
        kb.do("v", lambda e: e.tensor_copy(out=qkvb_fm[:], in_=bank32(2, 48)), inc="ve")

        for jj in range(2 * NJ):
            half, j = jj // NJ, jj % NJ
            kk, lc = j // 11, j % 11
            base = half * FI + 512 * lc + kk
            src = comb_ap(sgu[0:16, :], base)
            b = 3 + (jj % 2)
            acquire("t", b)
            kb.do("t", lambda e, s=src, bb=b: e.transpose(
                PSH[:, 1024 * bb: 1024 * bb + 16], s, idh[0:16, 0:16]), inc="pe")
            kb.w("v", "pe")
            kb.do("v", lambda e, jj=jj, bb=b: e.tensor_scalar(
                out=scl_g[:, 16 * jj:16 * jj + 16], in0=PSH[:, 1024 * bb: 1024 * bb + 16],
                scalar1=cc(C126), scalar2=None, op0=ALU.mult), inc="ve")
            release(b, "ve")
        for oc in range(16):
            b = 3 + (oc % 2)
            acquire("t", b)
            kb.do("t", lambda e, oc=oc, bb=b: e.transpose(
                PSH[:, 1024 * bb: 1024 * bb + 44], sdn[0:44, oc * 128:(oc + 1) * 128],
                idh[0:44, 0:44]), inc="pe")
            kb.w("v", "pe")
            kb.do("v", lambda e, oc=oc, bb=b: e.tensor_scalar(
                out=scl_d[:, 44 * oc:44 * oc + 44], in0=PSH[:, 1024 * bb: 1024 * bb + 44],
                scalar1=cc(C126), scalar2=None, op0=ALU.mult), inc="ve")
            release(b, "ve")

        kb.barrier()
        ctx0.close()

        ctxB0 = ExitStack()
        b0sb = lambda name, shape, dt: ctxB0.enter_context(nc.sbuf_tensor(name, shape, dt))
        pkb = (b0sb("pk0", [128, DWH], I32), b0sb("pk1", [128, DWH], I32))
        ubuf = b0sb("ubuf", [128, 4 * DWH], I32)
        sbuf_ = b0sb("sbuf_", [128, 4 * DWH], I32)
        deqb = (b0sb("deq0", [128, 8 * DWH], BF16), b0sb("deq1", [128, 8 * DWH], BF16))
        xstage = (b0sb("xst0", [128, DM], BF16), b0sb("xst1", [128, DM], BF16))
        ctxB = ExitStack()
        bsb = lambda name, shape, dt: ctxB.enter_context(nc.sbuf_tensor(name, shape, dt))
        x0a = bsb("x0a", [128, DM], F32)
        x0b = bsb("x0b", [128, DM], F32)
        x0bufs = (x0a, x0b)
        A_sb = bsb("A_sb", [128, 16 * 512], BF16)
        QT = (bsb("QT0", [128, S], BF16), bsb("QT1", [128, S], BF16))
        KT = (bsb("KT0", [128, S], BF16), bsb("KT1", [128, S], BF16))
        Vh = (bsb("Vh0", [128, 16 * 129], BF16), bsb("Vh1", [128, 16 * 129], BF16))
        WqTst = bsb("WqTst", [128, 16 * 128], BF16)
        WkTst = bsb("WkTst", [128, 16 * 128], BF16)
        WvTst = bsb("WvTst", [128, 16 * 128], BF16)
        wnat_q = bsb("wnat_q", [128, DM], BF16)
        wnat_k = bsb("wnat_k", [128, DM], BF16)
        wnat_v = bsb("wnat_v", [128, DM], BF16)
        ctx_tm = (bsb("ctm0", [128, 128], BF16), bsb("ctm1", [128, 128], BF16))
        ctxTst = (bsb("cts0", [128, S], BF16), bsb("cts1", [128, S], BF16))

        for p in range(2):
            kb.do("v", lambda e, p=p: e.memset(Vh[p][:], 1.0), inc="ve")

        def wnat_fetch(h):
            kb.dma("g", wnat_q[:], wqkv_h[128 * h:128 * (h + 1), :], inc="dg")
            kb.dma("g", wnat_k[:], wqkv_h[DM + 128 * h:DM + 128 * (h + 1), :], inc="dg")
            kb.dma("g", wnat_v[:], wqkv_h[2 * DM + 128 * h:2 * DM + 128 * (h + 1), :],
                   inc="dg")
            return kb.cnt["dg"]

        wnat_ready = wnat_fetch(0)

        x0_free = [("ve", 0), ("ve", 0)]
        for tc in range(NT):
            xb = x0bufs[tc % 2]
            sem, v = x0_free[tc % 2]
            kb.w("y", sem, v)
            ld = kb.dma("y", xb[:], x_h[tc * 128:(tc + 1) * 128, :], inc="ds")
            kb.w("v", "ds", ld)
            kb.w("s", "ds", ld)
            kb.w("v", "pe")
            ln_stats(xb[:])
            ln_chain(CE5)
            kb.do("v", lambda e, x=xb: e.tensor_tensor(
                out=xn_tm[:, 0:DM], in0=x[:], in1=svb(8, DM), op=ALU.mult))
            kb.do("v", lambda e: e.tensor_tensor(
                out=xn_tm[:, 0:DM], in0=xn_tm[:, 0:DM], in1=svb(9, DM), op=ALU.add),
                inc="ve")
            x0_free[tc % 2] = ("ve", kb.cnt["ve"])
            emit_xn_transposes(tc, permuted=False)

        maybe_stop("A", [xnT[:]])
        if DBG_BAR >= 1:
            kb.barrier()
        def prep_slice(h, s, state):
            hp = h % 2
            if s == 0:
                kb.w("t", "dg", state["wnat_ready"])
                for src, dstst in ((wnat_q, WqTst), (wnat_k, WkTst)):
                    for halfb in range(2):
                        b = halfb
                        acquire("t", b)
                        for i in range(8):
                            dc = halfb * 8 + i
                            kb.do("t", lambda e, d=PSB[:, 1024 * b + i * 128:
                                                       1024 * b + (i + 1) * 128],
                                  s2=src[:, dc * 128:(dc + 1) * 128]:
                                  e.transpose(d, s2, idb[:]),
                                  inc=("pe" if i == 7 else None))
                        kb.w("v", "pe")
                        kb.do("v", lambda e, d=dstst[:, halfb * 1024:(halfb + 1) * 1024],
                              s2=bankbf(b): e.tensor_copy(out=d, in_=s2), inc="ve")
                        release(b, "ve")
            elif s in (1, 2):
                wst, dst, bias_col = ((WqTst, QT[hp], h) if s == 1
                                      else (WkTst, KT[hp], 16 + h))
                kb.wf("t", "ve")
                for tb in range(4):
                    b = 2 + tb % 2
                    acquire("t", b)
                    for dc in range(ND):
                        kb.do("t", lambda e, bb=b, w=wst[:, dc * 128:(dc + 1) * 128],
                              r=xnT_rhs(dc, tb * 512, 512), st_=(dc == 0), sp=(dc == 15):
                              e.matmul(bank32(bb), w, r, start=st_, stop=sp),
                              inc=("pe" if dc == 15 else None))
                    kb.w("v", "pe")
                    kb.do("v", lambda e, bb=b, d=dst[:, tb * 512:(tb + 1) * 512],
                          bcol=qkvb_fm[:, bias_col:bias_col + 1]: e.tensor_scalar(
                              out=d, in0=bank32(bb), scalar1=bcol, scalar2=None,
                              op0=ALU.add), inc="ve")
                    release(b, "ve")
                if s == 2:
                    for halfb in range(2):
                        b = halfb
                        acquire("t", b)
                        for i in range(8):
                            dc = halfb * 8 + i
                            kb.do("t", lambda e, d=PSB[:, 1024 * b + i * 128:
                                                       1024 * b + (i + 1) * 128],
                                  s2=wnat_v[:, dc * 128:(dc + 1) * 128]:
                                  e.transpose(d, s2, idb[:]),
                                  inc=("pe" if i == 7 else None))
                        kb.w("v", "pe")
                        kb.do("v", lambda e, d=WvTst[:, halfb * 1024:(halfb + 1) * 1024],
                              s2=bankbf(b): e.tensor_copy(out=d, in_=s2), inc="ve")
                        release(b, "ve")
                    state["wnat_done_pe"] = kb.cnt["pe"]
            else:
                kb.wf("t", "ve")
                for tb in range(4):
                    b = 2 + tb % 2
                    acquire("t", b)
                    for dc in range(ND):
                        kb.do("t", lambda e, bb=b, w=WvTst[:, dc * 128:(dc + 1) * 128],
                              r=xnT_rhs(dc, tb * 512, 512), st_=(dc == 0), sp=(dc == 15):
                              e.matmul(bank32(bb), w, r, start=st_, stop=sp),
                              inc=("pe" if dc == 15 else None))
                    kb.w("v", "pe")
                    kb.do("v", lambda e, bb=b, d=xn_tm[:, tb * 512:(tb + 1) * 512],
                          bcol=qkvb_fm[:, 32 + h:33 + h]: e.tensor_scalar(
                              out=d, in0=bank32(bb), scalar1=bcol, scalar2=None,
                              op0=ALU.add), inc="ve")
                    release(b, "ve")
                kb.w("t", "ve")
                for halfb in range(2):
                    b = halfb
                    acquire("t", b)
                    for i in range(8):
                        kc = halfb * 8 + i
                        kb.do("t", lambda e, d=PSB[:, 1024 * b + i * 128:
                                                   1024 * b + (i + 1) * 128],
                              s2=xn_tm[:, kc * 128:(kc + 1) * 128]:
                              e.transpose(d, s2, idb[:]),
                              inc=("pe" if i == 7 else None))
                    kb.w("v", "pe")
                    for i in range(8):
                        kc = halfb * 8 + i
                        kb.do("v", lambda e, d=Vh[hp][:, kc * 129:kc * 129 + 128],
                              s2=PSB[:, 1024 * b + i * 128: 1024 * b + (i + 1) * 128]:
                              e.tensor_copy(out=d, in_=s2),
                              inc=("ve" if i == 7 else None))
                    release(b, "ve")

        deq_state = {"nat_chunks_done": 0, "gate_rb_emitted": 0, "deq_free": [("dg", 0),
                     ("dg", 0)], "pk_free": [("ve", 0), ("ve", 0)], "pend": [],
                     "xst_free": [0, 0], "down_xbar_emitted": 0}

        def emit_pack_load(kind, idx, c):
            sem, v = deq_state["pk_free"][c % 2]
            kb.w("g", sem, v)
            if kind == "g":
                half, j = idx // NJ, idx % NJ
                kk, lc = j // 11, j % 11
                base = half * FI + 512 * lc + kk
                src = gup_h[base:base + 512, :].rearrange("(c e) w -> c e w", e=8)[:, 0:8:4, :]
                dst = pkb[c % 2][:, 0:GW]
            else:
                oc, hf = idx // 2, idx % 2
                src = dnp_h[oc * 128:(oc + 1) * 128, hf * DWH:(hf + 1) * DWH]
                dst = pkb[c % 2][:, 0:DWH]
            return kb.dma("g", dst, src, inc="dg")

        def emit_dequant_unit(kind, idx, c, ld):
            W = GW if kind == "g" else DWH
            pk = pkb[c % 2][:, 0:W]
            sem, v = deq_state["deq_free"][c % 2]
            kb.w("v", sem, v)
            kb.w("v", "dg", ld)
            for k in range(4):
                sh = 4 * k - 6
                op0 = ALU.logical_shift_right if sh >= 0 else ALU.logical_shift_left
                kb.do("v", lambda e, k=k, a=abs(sh), o=op0, pk=pk, W=W: e.tensor_scalar(
                    out=ubuf[:, k * W:(k + 1) * W], in0=pk, scalar1=a, scalar2=0x01C001C0,
                    op0=o, op1=ALU.bitwise_and))
                kb.do("v", lambda e, k=k, pk=pk, W=W: e.tensor_scalar(
                    out=sbuf_[:, k * W:(k + 1) * W], in0=pk, scalar1=12 - 4 * k,
                    scalar2=0x80008000 - 2 ** 32, op0=ALU.logical_shift_left,
                    op1=ALU.bitwise_and))
            deq_state["pk_free"][c % 2] = ("ve", kb.cnt["ve"] + 1)
            kb.do("v", lambda e, W=W: e.tensor_tensor(
                out=ubuf[:, 0:4 * W], in0=ubuf[:, 0:4 * W], in1=sbuf_[:, 0:4 * W],
                op=ALU.bitwise_or), inc="ve")
            if kind == "g":
                scl = scl_g[:, idx * 16:(idx + 1) * 16]
                nq = 16
            else:
                oc, hf = idx // 2, idx % 2
                scl = scl_d[:, oc * 44 + hf * 22: oc * 44 + (hf + 1) * 22]
                nq = 22
            sclap = scl.unsqueeze(1).unsqueeze(3).broadcast_to([128, 4, nq, 32])
            dq = deqb[c % 2][:, 0:8 * W]
            kb.do("v", lambda e, W=W, nq=nq, s2=sclap, d=dq: e.tensor_tensor(
                out=d.rearrange("p (k q r) -> p k q r", k=4, q=nq),
                in0=ubuf[:, 0:4 * W].bitcast(BF16).rearrange(
                    "p (k q r) -> p k q r", k=4, q=nq),
                in1=s2, op=ALU.mult), inc="ve")
            mult_done = kb.flush("ve")
            kb.w("g", "ve", mult_done)
            if kind == "g":
                dst = wgu_nat[idx * 128:(idx + 1) * 128, :]
                wr = kb.dma("g", dst, dq, inc="dg")
            else:
                oc, hf = idx // 2, idx % 2
                dst = wdn_nat[oc * 128:(oc + 1) * 128, :].rearrange(
                    "p (k j) -> p k j", k=4)[:, :, hf * 704:(hf + 1) * 704]
                wr = kb.dma("g", dst, dq.rearrange("p (k j) -> p k j", k=4), inc="dg")
            deq_state["deq_free"][c % 2] = ("dg", wr)
            return wr

        deq_counter = [0]

        def dequant_package(units):
            pend = deq_state["pend"]
            for u in units:
                pend.append(u)
            while len(pend) > 1:
                kind, idx = pend.pop(0)
                c = deq_counter[0]
                if "ld%d" % c not in deq_state:
                    deq_state["ld%d" % c] = emit_pack_load(kind, idx, c)
                ld = deq_state.pop("ld%d" % c)
                if pend:
                    k2, i2 = pend[0]
                    deq_state["ld%d" % (c + 1)] = emit_pack_load(k2, i2, c + 1)
                emit_dequant_unit(kind, idx, c, ld)
                deq_counter[0] += 1
                if kind == "g":
                    deq_state["nat_chunks_done"] += 1
                    maybe_emit_gate_xbar()

        def flush_dequant():
            pend = deq_state["pend"]
            while pend:
                kind, idx = pend.pop(0)
                c = deq_counter[0]
                if "ld%d" % c not in deq_state:
                    deq_state["ld%d" % c] = emit_pack_load(kind, idx, c)
                ld = deq_state.pop("ld%d" % c)
                if pend:
                    k2, i2 = pend[0]
                    deq_state["ld%d" % (c + 1)] = emit_pack_load(k2, i2, c + 1)
                emit_dequant_unit(kind, idx, c, ld)
                deq_counter[0] += 1
                if kind == "g":
                    deq_state["nat_chunks_done"] += 1
                    maybe_emit_gate_xbar()

        xbar_rows = [(rb * 2048, min(2048, 2 * FI - rb * 2048)) for rb in range(6)]

        def maybe_emit_gate_xbar():
            while deq_state["gate_rb_emitted"] < 6:
                rb = deq_state["gate_rb_emitted"]
                r0, nr = xbar_rows[rb]
                need = (r0 + nr) // 128
                if deq_state["nat_chunks_done"] < need:
                    return
                njb = nr // 128
                for dc in range(ND):
                    kb.w("y", "dg")
                    kb.w("y", "ds", deq_state["xst_free"][dc % 2])
                    ld = kb.dmat("y", xstage[dc % 2][:, 0:nr],
                                 wgu_nat[r0:r0 + nr, dc * 128:(dc + 1) * 128], inc="ds")
                    kb.w("y", "ds", ld)
                    dst = wgu_t[dc, r0 // 128: r0 // 128 + njb, :, :].transpose([1, 0, 2])
                    wr = kb.dma("y", dst,
                                xstage[dc % 2][:, 0:nr].rearrange("p (j f) -> p j f", f=128),
                                inc="ds")
                    deq_state["xst_free"][dc % 2] = wr
                deq_state["gate_rb_emitted"] += 1

        gate_per_pkg = [3] * 4 + [7] * 4 + [6] * 8
        gate_start = [sum(gate_per_pkg[:i]) for i in range(16)]

        def emit_down_xbar(n):
            while deq_state["down_xbar_emitted"] < min(n, NJ):
                j = deq_state["down_xbar_emitted"]
                kb.w("y", "dg")
                kb.w("y", "ds", deq_state["xst_free"][j % 2])
                ld = kb.dmat("y", xstage[j % 2][:],
                             wdn_nat[0:DM, j * 128:(j + 1) * 128], inc="ds")
                kb.w("y", "ds", ld)
                wr = kb.dma("y", wdn_t[j].transpose([1, 0, 2]),
                            xstage[j % 2][:].rearrange("p (o f) -> p o f", f=512),
                            inc="ds")
                deq_state["xst_free"][j % 2] = wr
                deq_state["down_xbar_emitted"] += 1

        def pkg_units(h):
            us = []
            if h < 4:
                us += [("d", i) for i in range(8 * h, 8 * h + 8)]
            us += [("g", jj) for jj in range(gate_start[h],
                                             gate_start[h] + gate_per_pkg[h])]
            return us

        state = {"wnat_ready": wnat_ready, "wnat_done_pe": 0}
        for s in range(4):
            prep_slice(0, s, state)
        maybe_stop("P0", [xnT[:], QT[0][:], KT[0][:], Vh[0][:]])
        kb.w("g", "pe", state["wnat_done_pe"])
        state["wnat_ready"] = wnat_fetch(1)

        ctxst_free = [0, 0]
        for h in range(H):
            hp = h % 2
            if DBG_BAR >= 2:
                kb.barrier()
            if DBG_SEQPREP and h > 0:
                for s_ in range(4):
                    prep_slice(h, s_, state)
            kb.w("v", "dg", ctxst_free[hp])
            for qb in range(4):
                for kc2 in range(8):
                    for d2 in range(2):
                        kc = 2 * kc2 + d2
                        b = 4 + d2
                        acquire("t", b)
                        kb.do("t", lambda e, bb=b, l=KT[hp][:, kc * 128:(kc + 1) * 128],
                              r=QT[hp][:, qb * 512:(qb + 1) * 512]:
                              e.matmul(bank32(bb), l, r, start=True, stop=True),
                              inc="pe")
                    kb.w("s", "pe")
                    kb.do("s", lambda e, d=A_sb[:, kc2 * 1024:(kc2 + 1) * 1024]:
                          e.activation(out=d, in_=PSa[:, 2048:3072], func=AF.Exp,
                                       scale=cc(CSM), bias=cc(CSH)), inc="ac")
                    release(4, "ac")
                    release(5, "ac")
                if h + 1 < H and not DBG_SEQPREP:
                    prep_slice(h + 1, qb, state)
                    if qb == 2 and h + 2 < H:
                        kb.w("g", "pe", state["wnat_done_pe"])
                        state["wnat_ready"] = wnat_fetch(h + 2)
                if DBG_BAR >= 3:
                    kb.barrier()
                kb.wf("t", "ac")
                kb.wf("t", "ve")
                for qc in range(4):
                    b = 6 + qc % 2
                    acquire("t", b)
                    for kc in range(16):
                        kb.do("t", lambda e, bb=b,
                              l=A_sb[:, kc * 512 + qc * 128: kc * 512 + (qc + 1) * 128],
                              r=Vh[hp][:, kc * 129:(kc + 1) * 129], st_=(kc == 0),
                              sp=(kc == 15):
                              e.matmul(PSa[:, 512 * bb: 512 * bb + 129], l, r,
                                       start=st_, stop=sp),
                              inc=("pe" if kc == 15 else None))
                    kb.w("v", "pe")
                    kb.do("v", lambda e, bb=b: e.reciprocal(
                        out=sv(10), in_=PSa[:, 512 * bb + 128: 512 * bb + 129]))
                    sp()
                    sp()
                    kb.do("v", lambda e, bb=b, d=ctx_tm[qc % 2][:]: e.tensor_tensor(
                        out=d, in0=PSa[:, 512 * bb: 512 * bb + 128], in1=svb(10, 128),
                        op=ALU.mult), inc="ve")
                    release(b, "ve")
                    b2 = qc % 2
                    acquire("t", b2)
                    kb.wf("t", "ve")
                    kb.do("t", lambda e, bb=b2, s2=ctx_tm[qc % 2][:]:
                          e.transpose(PSB[:, 1024 * bb: 1024 * bb + 128], s2, idb[:]),
                          inc="pe")
                    kb.w("v", "pe")
                    kb.do("v", lambda e, bb=b2,
                          d=ctxTst[hp][:, (qb * 4 + qc) * 128:(qb * 4 + qc + 1) * 128]:
                          e.tensor_copy(out=d, in_=PSB[:, 1024 * bb: 1024 * bb + 128]),
                          inc="ve")
                    release(b2, "ve")
            dequant_package(pkg_units(h))
            if h >= 4:
                emit_down_xbar(4 * (h - 3))
            kb.wf("g", "ve")
            kb.dma("g", ctxT_hbm[h], ctxTst[hp][:], inc="dg")
            ctxst_free[hp] = kb.cnt["dg"]
            maybe_stop("H%d" % h, [ctxTst[hp][:], A_sb[:], QT[hp][:], KT[hp][:],
                                   Vh[hp][:]])
        flush_dequant()
        emit_down_xbar(NJ)


        for sem in ("ve", "ac"):
            kb.flush(sem)
        snapB = dict(kb.cnt)
        for st in ENGS:
            for sem in ("pe", "ve", "ac", "dg"):
                kb.w(st, sem, snapB[sem])
        ctxB.close()

        ctxE = ExitStack()
        esb = lambda name, shape, dt: ctxE.enter_context(nc.sbuf_tensor(name, shape, dt))
        WoT = esb("WoT", [128, ND * DM], BF16)
        woslab_s = esb("wos0", [128, S], BF16)
        woslab = (woslab_s, woslab_s)
        woslab_free = [("ve", 0), ("ve", 0)]
        ctxslab_s = esb("cxs0", [128, 16 * 128], BF16)
        ctxslab = (ctxslab_s, ctxslab_s)
        outbb = esb("outbb", [128, DM], BF16)
        tmpE = esb("tmpE", [128, DM], BF16)
        x1st = esb("x1st", [128, DM], BF16)
        x0e_s = esb("x0e0", [128, DM], BF16)
        x0e = (x0e_s, x0e_s)

        kb.dma("g", outbb[:], bo_h.rearrange("(a b) -> a b", a=1).broadcast_to([128, DM]),
               inc="dg")

        for blk in range(16):
            og, sub = blk // 4, blk % 4
            sem, v = woslab_free[blk % 2]
            kb.w("g", sem, v)
            ld = kb.dma("g", woslab[blk % 2][:],
                        wo_h[og * 512 + sub * 128: og * 512 + (sub + 1) * 128, :], inc="dg")
            kb.w("t", "dg", ld)
            for halfb in range(2):
                b = halfb
                acquire("t", b)
                for i in range(8):
                    dc = halfb * 8 + i
                    kb.do("t", lambda e, d=PSB[:, 1024 * b + i * 128: 1024 * b + (i + 1) * 128],
                          s2=woslab[blk % 2][:, dc * 128:(dc + 1) * 128]:
                          e.transpose(d, s2, idb[:]), inc=("pe" if i == 7 else None))
                kb.w("v", "pe")
                dst = WoT[:].rearrange("p (d o) -> p d o", o=DM)[
                    :, halfb * 8: halfb * 8 + 8, og * 512 + sub * 128: og * 512 + (sub + 1) * 128]
                kb.do("v", lambda e, d=dst, s2=bankbf(b): e.tensor_copy(out=d, in_=s2),
                      inc="ve")
                release(b, "ve")
            woslab_free[0] = woslab_free[1] = ("ve", kb.cnt["ve"])
        wot_ready = kb.flush("ve")

        ctxslab_free = [("pe", 0), ("pe", 0)]
        x0e_free = [("ve", 0), ("ve", 0)]
        x1st_free = [0]
        for tc in range(NT):
            sem, v = ctxslab_free[tc % 2]
            kb.w("y", sem, v)
            src = ctxT_hbm[:, :, tc * 128:(tc + 1) * 128].transpose([1, 0, 2])
            ld = kb.dma("y", ctxslab[tc % 2][:].rearrange("p (h f) -> p h f", f=128), src,
                        inc="ds")
            sem, v = x0e_free[tc % 2]
            kb.w("g", sem, v)
            ldx = kb.dma("g", x0e[tc % 2][:], x_h[tc * 128:(tc + 1) * 128, :], inc="dg")
            kb.w("t", "ds", ld)
            kb.w("t", "ve", wot_ready)
            base_bank = 4 * (tc % 2)
            for og in range(4):
                b = base_bank + og
                acquire("t", b)
                for dc in range(ND):
                    kb.do("t", lambda e, bb=b,
                          l=ctxslab[tc % 2][:, dc * 128:(dc + 1) * 128],
                          r=WoT[:, dc * DM + og * 512: dc * DM + (og + 1) * 512],
                          st_=(dc == 0), sp=(dc == 15):
                          e.matmul(bank32(bb), l, r, start=st_, stop=sp),
                          inc=("pe" if dc == 15 else None))
            ctxslab_free[0] = ctxslab_free[1] = ("pe", kb.cnt["pe"])
            kb.w("v", "pe")
            kb.w("v", "dg")
            kb.w("v", "ac")
            kb.do("v", lambda e, bb=base_bank: e.tensor_tensor(
                out=tmpE[:], in0=PSa[:, 512 * bb: 512 * bb + 2048], in1=outbb[:],
                op=ALU.add), inc="ve")
            for og in range(4):
                release(base_bank + og, "ve")
            ln_stats(tmpE[:])
            ln_chain(CE5)
            kb.w("v", "dg", ldx)
            kb.do("v", lambda e, x=x0e[tc % 2]: e.tensor_tensor(
                out=dump[:], in0=x[:], in1=svb(9, DM), op=ALU.add))
            kb.w("v", "dg", x1st_free[0])
            kb.do("v", lambda e: e.tensor_tensor(
                out=x1st[:], in0=tmpE[:], in1=svb(8, DM), op=ALU.mult))
            kb.do("v", lambda e: e.tensor_tensor(
                out=x1st[:], in0=x1st[:], in1=dump[:], op=ALU.add), inc="ve")
            x0e_free[0] = x0e_free[1] = ("ve", kb.cnt["ve"])
            kb.wf("g", "ve")
            kb.dma("g", x1_hbm[tc * 128:(tc + 1) * 128, :], x1st[:], inc="dg")
            x1st_free[0] = kb.cnt["dg"]

        maybe_stop("OP", [x1_hbm[0:S, :]])
        kb.barrier()
        ctxE.close()
        ctxB0.close()

        ctxC = ExitStack()
        csb = lambda name, shape, dt: ctxC.enter_context(nc.sbuf_tensor(name, shape, dt))
        h_sb = csb("h_sb", [128, NJ * 512], BF16)
        ybuf = csb("ybuf", [128, 4 * DM], BF16)
        wguTb = (csb("wgt0", [128, 16 * 128], BF16), csb("wgt1", [128, 16 * 128], BF16),
                 csb("wgt2", [128, 16 * 128], BF16), csb("wgt3", [128, 16 * 128], BF16))
        wdnTb = (csb("wdt0", [128, 512], BF16), csb("wdt1", [128, 512], BF16))
        sg_sb = (csb("sg0", [128, 512], BF16), csb("sg1", [128, 512], BF16))
        x1b = (csb("x1b0", [128, DM], BF16), csb("x1b1", [128, DM], BF16))
        outst = (csb("os0", [128, DM], F32), csb("os1", [128, DM], F32))

        x1_free = [("ve", 0), ("ve", 0)]
        for tc in range(NT):
            sem, v = x1_free[tc % 2]
            kb.w("y", sem, v)
            ld = kb.dma("y", x1b[tc % 2][:], x1_hbm[tc * 128:(tc + 1) * 128, :], inc="ds")
            kb.w("v", "ds", ld)
            kb.w("s", "ds", ld)
            kb.w("v", "pe")
            ln_stats(x1b[tc % 2][:])
            ln_chain(CE5)
            kb.do("v", lambda e, x=x1b[tc % 2]: e.tensor_tensor(
                out=xn_tm[:, 0:DM], in0=x[:], in1=svb(8, DM), op=ALU.mult))
            kb.do("v", lambda e: e.tensor_tensor(
                out=xn_tm[:, 0:DM], in0=xn_tm[:, 0:DM], in1=svb(9, DM), op=ALU.add),
                inc="ve")
            x1_free[tc % 2] = ("ve", kb.cnt["ve"])
            emit_xn_transposes(tc, permuted=True)

        maybe_stop("LN3", [xnT[:], x1_hbm[0:S, :]])
        wgu_free = [("pe", 0)] * 4
        wdn_free = [("pe", 0), ("pe", 0)]
        sg_free = [0, 0]
        outst_free = [0, 0]
        for g in range(4):
            kb.wf("t", "ve")
            kb.w("t", "dg")
            kb.w("v", "pe")
            kb.w("s", "pe")
            if g == 0:
                kb.w("y", "ds")
            for j in range(NJ):
                for half in range(2):
                    jj = half * NJ + j
                    sl = (2 * j + half) % 4
                    sem, v = wgu_free[sl]
                    kb.w("y", sem, v)
                    src = wgu_t[:, jj, :, :].transpose([1, 0, 2])
                    ld = kb.dma("y", wguTb[sl][:].rearrange("p (d f) -> p d f", f=128),
                                src, inc="ds")
                    b = 2 * (j % 3) + half
                    acquire("t", b)
                    kb.w("t", "ds", ld)
                    for dc in range(ND):
                        kb.do("t", lambda e, bb=b, l=wguTb[sl][:, dc * 128:(dc + 1) * 128],
                              r=xnT_rhs(dc, g * 512, 512), st_=(dc == 0), sp=(dc == 15):
                              e.matmul(bank32(bb), l, r, start=st_, stop=sp),
                              inc=("pe" if dc == 15 else None))
                    wgu_free[sl] = ("pe", kb.cnt["pe"])
                bg = 2 * (j % 3)
                kb.w("s", "pe")
                kb.w("s", "ve", sg_free[j % 2])
                kb.do("s", lambda e, bb=bg, d=sg_sb[j % 2][:]: e.activation(
                    out=d, in_=bank32(bb), func=AF.Silu), inc="ac")
                kb.wf("v", "ac")
                kb.do("v", lambda e, bb=bg, d=h_sb[:, j * 512:(j + 1) * 512],
                      s2=sg_sb[j % 2][:]: e.tensor_tensor(
                          out=d, in0=s2, in1=bank32(bb + 1), op=ALU.mult), inc="ve")
                sg_free[j % 2] = kb.cnt["ve"]
                release(bg, "ac")
                release(bg + 1, "ve")
            if g == 0:
                maybe_stop("G0", [h_sb[:], wgu_t[0:ND, :, :, :], x1_hbm[0:S, :]])
            kb.wf("t", "ve")
            for og in range(4):
                for b in (4, 5, 6, 7):
                    acquire("t", b)
                for j in range(NJ):
                    sl = j % 2
                    sem, v = wdn_free[sl]
                    kb.w("y", sem, v)
                    ld = kb.dma("y", wdnTb[sl][:], wdn_t[j, og], inc="ds")
                    kb.w("t", "ds", ld)
                    for tc4 in range(4):
                        kb.do("t", lambda e, bb=4 + tc4,
                              l=h_sb[:, j * 512 + tc4 * 128: j * 512 + (tc4 + 1) * 128],
                              r=wdnTb[sl][:], st_=(j == 0), sp=(j == NJ - 1):
                              e.matmul(bank32(bb), l, r, start=st_, stop=sp),
                              inc=("pe" if tc4 == 3 else None))
                    wdn_free[sl] = ("pe", kb.cnt["pe"])
                kb.w("v", "pe")
                for tc4 in range(4):
                    kb.do("v", lambda e, bb=4 + tc4,
                          d=ybuf[:, tc4 * DM + og * 512: tc4 * DM + (og + 1) * 512]:
                          e.tensor_copy(out=d, in_=bank32(bb)),
                          inc=("ve" if tc4 == 3 else None))
                for b in (4, 5, 6, 7):
                    release(b, "ve")
            if g == 0:
                maybe_stop("D0", [ybuf[:], h_sb[:], x1_hbm[0:S, :]])
            for tc4 in range(4):
                tc = g * 4 + tc4
                yrow = ybuf[:, tc4 * DM:(tc4 + 1) * DM]
                kb.w("v", "ve")
                ln_stats(yrow)
                ln_chain(CE5, extra_eps_col=CE11)
                sem, v = x1_free[tc % 2]
                kb.w("y", sem, v)
                ldx = kb.dma("y", x1b[tc % 2][:], x1_hbm[tc * 128:(tc + 1) * 128, :],
                             inc="ds")
                kb.w("v", "ds", ldx)
                kb.do("v", lambda e, x=x1b[tc % 2]: e.tensor_tensor(
                    out=dump[:], in0=x[:], in1=svb(9, DM), op=ALU.add))
                kb.w("v", "so", outst_free[tc4 % 2])
                kb.do("v", lambda e, y=yrow, o=outst[tc4 % 2]: e.tensor_tensor(
                    out=o[:], in0=y, in1=svb(8, DM), op=ALU.mult))
                kb.do("v", lambda e, o=outst[tc4 % 2]: e.tensor_tensor(
                    out=o[:], in0=o[:], in1=dump[:], op=ALU.add), inc="ve")
                x1_free[tc % 2] = ("ve", kb.cnt["ve"])
                kb.wf("y", "ve")
                kb.dma("y", out_h[tc * 128:(tc + 1) * 128, :], outst[tc4 % 2][:], inc="so")
                outst_free[tc4 % 2] = kb.cnt["so"]

        kb.w("y", "so", kb.cnt["so"])
        kb.barrier()

    except StopEmission:
        kb.w("g", "dg")

    with nc.Block() as block:
        @block.tensor
        def _(e):
            for f in kb.ops["t"]:
                f(e)

        @block.vector
        def _(e):
            for f in kb.ops["v"]:
                f(e)

        @block.scalar
        def _(e):
            for f in kb.ops["s"]:
                f(e)

        @block.sync
        def _(e):
            for f in kb.ops["y"]:
                f(e)

        @block.gpsimd
        def _(e):
            for f in kb.ops["g"]:
                f(e)

    pctx.pop_all()
    return nc


def kernel(L0_attn_hidden_states, L0_attn_Wqkv_w, L0_attn_Wqkv_b, L0_attn_out_w,
           L0_attn_out_b, L0_ffn_gate_up_packed, L0_ffn_gate_up_scales,
           L0_ffn_down_packed, L0_ffn_down_scales):
    import ml_dtypes

    nc = build_core_program()

    x = np.ascontiguousarray(np.asarray(L0_attn_hidden_states, dtype=np.float32))
    shared = {
        "wqkv": np.ascontiguousarray(np.asarray(L0_attn_Wqkv_w, dtype=np.float32)),
        "bqkv": np.ascontiguousarray(np.asarray(L0_attn_Wqkv_b, dtype=np.float32)),
        "wo": np.ascontiguousarray(np.asarray(L0_attn_out_w, dtype=np.float32)),
        "bo": np.ascontiguousarray(np.asarray(L0_attn_out_b, dtype=np.float32)),
        "gup": np.ascontiguousarray(np.pad(np.asarray(L0_ffn_gate_up_packed, dtype=np.int32), ((0, 8), (0, 0)))),
        "gus": np.ascontiguousarray(np.asarray(L0_ffn_gate_up_scales, dtype=np.float16)),
        "dnp": np.ascontiguousarray(np.asarray(L0_ffn_down_packed, dtype=np.int32)),
        "dns": np.ascontiguousarray(np.asarray(L0_ffn_down_scales, dtype=np.float16)),
        "idb": np.eye(128, dtype=ml_dtypes.bfloat16),
        "idh": np.eye(128, dtype=np.float16),
        "idf": np.eye(128, dtype=np.float32),
    }
    in_maps = [dict(shared, x=np.ascontiguousarray(x[b])) for b in range(8)]
    res = run_bass_kernel_spmd(nc, in_maps, core_ids=list(range(8)))
    out = np.stack([np.asarray(r["out"], dtype=np.float32) for r in res.results], axis=0)
    return out


# version 15 (final)
# speedup vs baseline: 1.1574x; 1.1574x over previous
import numpy as np
from contextlib import ExitStack

import concourse.bass as bass
import concourse.mybir as mybir
from concourse.bass_utils import run_bass_kernel_spmd

F32 = mybir.dt.float32
BF16 = mybir.dt.bfloat16
F16 = mybir.dt.float16
I32 = mybir.dt.int32
AF = mybir.ActivationFunctionType
ALU = mybir.AluOpType
AXX = mybir.AxisListType.X

S = 2048
DM = 2048
H = 16
FI = 5632
NT = S // 128
ND = DM // 128
GW = DM // 8
DWH = (FI // 8) // 2
NJ = FI // 128

ENGS = ("t", "v", "s", "y", "g")
import os
DBG_BAR = int(os.environ.get("KDBG_BAR", "0"))
DBG_SEQPREP = int(os.environ.get("KDBG_SEQPREP", "0"))
KSTOP = os.environ.get("KSTOP", "")


class StopEmission(Exception):
    pass


class KB:
    def __init__(self, nc):
        self.nc = nc
        self.ops = {e: [] for e in ENGS}
        self.cnt = {}
        self.semh = {}
        for name in ("pe", "ve", "ac", "dg", "ds", "so"):
            self.cnt[name] = 0
            self.semh[name] = nc.alloc_semaphore(name)
        self.dirty = {}
        self.flushops = {}

    def w(self, st, sem, val=None):
        v = self.cnt[sem] if val is None else val
        if v <= 0:
            return
        h = self.semh[sem]
        self.ops[st].append(lambda e, h=h, v=v: e.wait_ge(h, v))

    def set_flushop(self, sem, st, fn):
        self.flushops[sem] = (st, fn)
        self.dirty[sem] = False

    def flush(self, sem):
        if not self.dirty.get(sem, False):
            return self.cnt[sem]
        st, fn = self.flushops[sem]
        self.dirty[sem] = False
        self.do(st, fn)
        return self.do(st, fn, inc=sem)

    def wf(self, st, sem, val=None):
        if sem in self.dirty:
            self.flush(sem)
        self.w(st, sem, val)

    def do(self, st, fn, inc=None, n=1):
        if inc is None:
            self.ops[st].append(lambda e, f=fn: f(e))
            return None
        if inc in self.dirty:
            self.dirty[inc] = True
        self.cnt[inc] += n
        h = self.semh[inc]
        self.ops[st].append(lambda e, f=fn, h=h, m=n: f(e).then_inc(h, m))
        return self.cnt[inc]

    def dma(self, st, out, in_, inc="ds"):
        return self.do(st, lambda e, o=out, i=in_: e.dma_start(out=o, in_=i), inc=inc, n=16)

    def dmat(self, st, out, in_, inc="ds"):
        return self.do(st, lambda e, o=out, i=in_: e.dma_start_transpose(out=o, in_=i),
                       inc=inc, n=16)

    def barrier(self):
        for sem in list(self.dirty):
            self.flush(sem)
        snap = dict(self.cnt)
        for st in ENGS:
            for sem in ("pe", "ve", "ac", "dg", "ds"):
                self.w(st, sem, snap[sem])


def build_core_program():
    nc = bass.Bass("TRN2", target_bir_lowering=False, detect_race_conditions=False)
    kb = KB(nc)

    x_h = nc.dram_tensor("x", [S, DM], F32, kind="ExternalInput").ap()
    wqkv_h = nc.dram_tensor("wqkv", [3 * DM, DM], F32, kind="ExternalInput").ap()
    bqkv_h = nc.dram_tensor("bqkv", [3 * DM], F32, kind="ExternalInput").ap()
    wo_h = nc.dram_tensor("wo", [DM, DM], F32, kind="ExternalInput").ap()
    bo_h = nc.dram_tensor("bo", [DM], F32, kind="ExternalInput").ap()
    gup_h = nc.dram_tensor("gup", [2 * FI + 8, GW], I32, kind="ExternalInput").ap()
    gus_h = nc.dram_tensor("gus", [16, 2 * FI], F16, kind="ExternalInput").ap()
    dnp_h = nc.dram_tensor("dnp", [DM, 2 * DWH], I32, kind="ExternalInput").ap()
    dns_h = nc.dram_tensor("dns", [44, DM], F16, kind="ExternalInput").ap()
    idb_h = nc.dram_tensor("idb", [128, 128], BF16, kind="ExternalInput").ap()
    idh_h = nc.dram_tensor("idh", [128, 128], F16, kind="ExternalInput").ap()
    idf_h = nc.dram_tensor("idf", [128, 128], F32, kind="ExternalInput").ap()
    out_h = nc.dram_tensor("out", [S, DM], F32, kind="ExternalOutput").ap()

    wgu_nat = nc.dram_tensor("wgu_nat", [2 * FI, DM], BF16, kind="Internal").ap()
    wdn_nat = nc.dram_tensor("wdn_nat", [DM, FI], BF16, kind="Internal").ap()
    wgu_t = nc.dram_tensor("wgu_t", [ND, 2 * NJ, 128, 128], BF16, kind="Internal").ap()
    wdn_t = nc.dram_tensor("wdn_t", [NJ, 4, 128, 512], BF16, kind="Internal").ap()
    ctxT_hbm = nc.dram_tensor("ctxT", [H, 128, S], BF16, kind="Internal").ap()
    x1_hbm = nc.dram_tensor("x1", [S, DM], BF16, kind="Internal").ap()

    pctx = ExitStack()
    psb = lambda name, shape, dt: pctx.enter_context(nc.sbuf_tensor(name, shape, dt))
    xnT = psb("xnT", [128, ND * S], BF16)
    xn_tm = psb("xn_tm", [128, DM + 8], BF16)
    dump = psb("dump", [128, DM], F32)
    cvec = psb("cvec", [128, 16], F32)
    svec = psb("svec", [128, 16], F32)
    scl_g = psb("scl_g", [128, 2 * NJ * 16], F32)
    scl_d = psb("scl_d", [128, 16 * 44], F32)
    qkvb_fm = psb("qkvb_fm", [128, 48], F32)
    idb = psb("idb_s", [128, 128], BF16)
    idh = psb("idh_s", [128, 128], F16)
    idf = psb("idf_s", [128, 128], F32)
    PS = pctx.enter_context(nc.psum_tensor("PS", [128, 4096], F32))
    PSa = PS[:]
    PSB = PSa.bitcast(BF16)
    PSH = PSa.bitcast(F16)

    def bank32(b, n=512):
        return PSa[:, 512 * b: 512 * b + n]

    def bankbf(b, n=1024):
        return PSB[:, 1024 * b: 1024 * b + n]

    C126, CSM, CSH, CN, CE5, CE11, CM1 = range(7)
    cc = lambda i: cvec[:, i:i + 1]
    sv = lambda i: svec[:, i:i + 1]
    svb = lambda i, n: svec[:, i:i + 1].broadcast_to([128, n])

    bank_free = [("pe", 0)] * 8

    def acquire(st, b):
        sem, v = bank_free[b]
        kb.w(st, sem, v)

    def release(b, sem):
        bank_free[b] = (sem, kb.cnt[sem])

    def sp():
        kb.do("v", lambda e: e.tensor_copy(out=sv(13), in_=cc(6)))

    def maybe_stop(tag, dumps):
        if KSTOP != tag:
            return
        kb.barrier()
        for i, ap in enumerate(dumps):
            dbg = nc.dram_tensor("kdbg_%d" % i, list(ap.shape), ap.dtype,
                                 kind="ExternalOutput").ap()
            kb.dma("g", dbg, ap, inc="dg")
        raise StopEmission

    def ln_chain(eps_col, extra_eps_col=None):
        kb.do("v", lambda e: e.tensor_scalar(out=sv(2), in0=sv(0), scalar1=cc(CN),
                                             scalar2=None, op0=ALU.mult))
        kb.do("v", lambda e: e.tensor_scalar(out=sv(3), in0=sv(1), scalar1=cc(CN),
                                             scalar2=None, op0=ALU.mult))
        kb.do("v", lambda e: e.tensor_tensor(out=sv(4), in0=sv(2), in1=sv(2), op=ALU.mult))
        sp()
        kb.do("v", lambda e: e.tensor_tensor(out=sv(5), in0=sv(3), in1=sv(4),
                                             op=ALU.subtract))
        sp()
        if extra_eps_col is None:
            kb.do("v", lambda e: e.tensor_scalar(out=sv(6), in0=sv(5),
                                                 scalar1=cc(eps_col), scalar2=None,
                                                 op0=ALU.add), inc="ve")
        else:
            kb.do("v", lambda e: e.scalar_tensor_tensor(out=sv(6), in0=sv(3),
                                                        scalar=cc(eps_col), in1=sv(5),
                                                        op0=ALU.mult, op1=ALU.add))
            sp()
            kb.do("v", lambda e: e.tensor_scalar(out=sv(6), in0=sv(6),
                                                 scalar1=cc(extra_eps_col), scalar2=None,
                                                 op0=ALU.add), inc="ve")
        kb.wf("s", "ve")
        kb.do("s", lambda e: e.activation(out=sv(7), in_=sv(6), func=AF.Sqrt), inc="ac")
        kb.wf("v", "ac")
        kb.do("v", lambda e: e.reciprocal(out=sv(8), in_=sv(7)))
        sp()
        kb.do("v", lambda e: e.scalar_tensor_tensor(out=sv(9), in0=sv(2), scalar=cc(CM1),
                                                    in1=sv(8), op0=ALU.mult, op1=ALU.mult))
        sp()
        sp()
        sp()

    def ln_stats(xbuf):
        kb.do("v", lambda e, x=xbuf: e.reduce_sum(sv(0), x, AXX), inc="ve")
        kb.wf("s", "ve")
        kb.do("s", lambda e, x=xbuf: e.activation(out=dump[:], in_=x, func=AF.Square,
                                                  accum_out=sv(1)), inc="ac")
        kb.wf("v", "ac")

    def comb_ap(base_ap, base):
        v = base_ap[:, base:base + 512]
        v = v.rearrange("p (c e) -> p c e", e=8)
        return v[:, :, 0:8:4]

    def xnT_rhs(dc, lo, n):
        return xnT[:, dc * S + lo: dc * S + lo + n]

    def emit_xn_transposes(tc, permuted):
        kb.wf("t", "ve")
        for halfb in range(2):
            b = halfb
            acquire("t", b)
            for i in range(8):
                dc = halfb * 8 + i
                if permuted:
                    kk, lc = dc // 4, dc % 4
                    src = comb_ap(xn_tm[:], 512 * lc + kk)
                else:
                    src = xn_tm[:, dc * 128:(dc + 1) * 128]
                kb.do("t", lambda e, d=PSB[:, 1024 * b + i * 128: 1024 * b + (i + 1) * 128],
                      s=src: e.transpose(d, s, idb[:]),
                      inc=("pe" if i == 7 else None))
            kb.w("v", "pe")
            dst = xnT[:].rearrange("p (d t) -> p d t", t=S)[:, halfb * 8: halfb * 8 + 8,
                                                           tc * 128:(tc + 1) * 128]
            kb.do("v", lambda e, d=dst, s=bankbf(b): e.tensor_copy(out=d, in_=s), inc="ve")
            release(b, "ve")

    try:
        ctx0 = ExitStack()
        sgu = ctx0.enter_context(nc.sbuf_tensor("sgu", [16, 2 * FI + 8], F16))
        sdn = ctx0.enter_context(nc.sbuf_tensor("sdn", [44, DM], F16))
        qkvb_n = ctx0.enter_context(nc.sbuf_tensor("qkvb_n", [48, 128], F32))

        kb.dma("y", idb[:], idb_h)
        kb.dma("y", idh[:], idh_h)
        kb.dma("y", idf[:], idf_h)
        kb.dma("g", sgu[:, 0:2 * FI], gus_h, inc="dg")
        kb.dma("g", sdn[:], dns_h, inc="dg")
        kb.dma("g", qkvb_n[:], bqkv_h.rearrange("(a b) -> a b", b=128), inc="dg")

        for col, val in ((C126, float(2.0 ** 126)),
                         (CSM, float(1.0 / np.float32(np.sqrt(np.float32(128.0))))),
                         (CSH, -12.0), (CN, 1.0 / 2048.0), (CE5, 1e-5), (CE11, 1e-11),
                         (CM1, -1.0)):
            kb.do("v", lambda e, c=col, v=val: e.memset(cvec[:, c:c + 1], v), inc="ve")

        kb.set_flushop("ve", "v", lambda e: e.tensor_copy(out=sv(15), in_=cc(6)))
        kb.set_flushop("ac", "s", lambda e: e.copy(out=sv(14), in_=cc(6)))
        kb.w("t", "ds")
        kb.w("t", "dg")
        kb.do("t", lambda e: e.transpose(bank32(2, 48), qkvb_n[:], idf[0:48, 0:48]), inc="pe")
        kb.w("v", "pe")
        kb.do("v", lambda e: e.tensor_copy(out=qkvb_fm[:], in_=bank32(2, 48)), inc="ve")

        for jj in range(2 * NJ):
            half, j = jj // NJ, jj % NJ
            kk, lc = j // 11, j % 11
            base = half * FI + 512 * lc + kk
            src = comb_ap(sgu[0:16, :], base)
            b = 3 + (jj % 2)
            acquire("t", b)
            kb.do("t", lambda e, s=src, bb=b: e.transpose(
                PSH[:, 1024 * bb: 1024 * bb + 16], s, idh[0:16, 0:16]), inc="pe")
            kb.w("v", "pe")
            kb.do("v", lambda e, jj=jj, bb=b: e.tensor_scalar(
                out=scl_g[:, 16 * jj:16 * jj + 16], in0=PSH[:, 1024 * bb: 1024 * bb + 16],
                scalar1=cc(C126), scalar2=None, op0=ALU.mult), inc="ve")
            release(b, "ve")
        for oc in range(16):
            b = 3 + (oc % 2)
            acquire("t", b)
            kb.do("t", lambda e, oc=oc, bb=b: e.transpose(
                PSH[:, 1024 * bb: 1024 * bb + 44], sdn[0:44, oc * 128:(oc + 1) * 128],
                idh[0:44, 0:44]), inc="pe")
            kb.w("v", "pe")
            kb.do("v", lambda e, oc=oc, bb=b: e.tensor_scalar(
                out=scl_d[:, 44 * oc:44 * oc + 44], in0=PSH[:, 1024 * bb: 1024 * bb + 44],
                scalar1=cc(C126), scalar2=None, op0=ALU.mult), inc="ve")
            release(b, "ve")

        kb.barrier()
        ctx0.close()

        ctxB0 = ExitStack()
        b0sb = lambda name, shape, dt: ctxB0.enter_context(nc.sbuf_tensor(name, shape, dt))
        pkb = (b0sb("pk0", [128, DWH], I32), b0sb("pk1", [128, DWH], I32))
        ubuf = b0sb("ubuf", [128, 4 * DWH], I32)
        sbuf_ = b0sb("sbuf_", [128, 4 * DWH], I32)
        deqb = (b0sb("deq0", [128, 8 * DWH], BF16), b0sb("deq1", [128, 8 * DWH], BF16))
        xstage = (b0sb("xst0", [128, DM], BF16), b0sb("xst1", [128, DM], BF16))
        ctxB = ExitStack()
        bsb = lambda name, shape, dt: ctxB.enter_context(nc.sbuf_tensor(name, shape, dt))
        x0a = bsb("x0a", [128, DM], F32)
        x0b = bsb("x0b", [128, DM], F32)
        x0bufs = (x0a, x0b)
        A_sb = bsb("A_sb", [128, 16 * 512], BF16)
        QT = (bsb("QT0", [128, S], BF16), bsb("QT1", [128, S], BF16))
        KT = (bsb("KT0", [128, S], BF16), bsb("KT1", [128, S], BF16))
        Vh = (bsb("Vh0", [128, 16 * 129], BF16), bsb("Vh1", [128, 16 * 129], BF16))
        WqTst = bsb("WqTst", [128, 16 * 128], BF16)
        WkTst = bsb("WkTst", [128, 16 * 128], BF16)
        WvTst = bsb("WvTst", [128, 16 * 128], BF16)
        wnat_q = bsb("wnat_q", [128, DM], BF16)
        wnat_k = bsb("wnat_k", [128, DM], BF16)
        wnat_v = bsb("wnat_v", [128, DM], BF16)
        ctx_tm = (bsb("ctm0", [128, 128], BF16), bsb("ctm1", [128, 128], BF16))
        ctxTst = (bsb("cts0", [128, S], BF16), bsb("cts1", [128, S], BF16))

        for p in range(2):
            kb.do("v", lambda e, p=p: e.memset(Vh[p][:], 1.0), inc="ve")

        def wnat_fetch(h):
            kb.dma("g", wnat_q[:], wqkv_h[128 * h:128 * (h + 1), :], inc="dg")
            kb.dma("g", wnat_k[:], wqkv_h[DM + 128 * h:DM + 128 * (h + 1), :], inc="dg")
            kb.dma("g", wnat_v[:], wqkv_h[2 * DM + 128 * h:2 * DM + 128 * (h + 1), :],
                   inc="dg")
            return kb.cnt["dg"]

        wnat_ready = wnat_fetch(0)

        x0_free = [("ve", 0), ("ve", 0)]
        for tc in range(NT):
            xb = x0bufs[tc % 2]
            sem, v = x0_free[tc % 2]
            kb.w("y", sem, v)
            ld = kb.dma("y", xb[:], x_h[tc * 128:(tc + 1) * 128, :], inc="ds")
            kb.w("v", "ds", ld)
            kb.w("s", "ds", ld)
            kb.w("v", "pe")
            ln_stats(xb[:])
            ln_chain(CE5)
            kb.do("v", lambda e, x=xb: e.tensor_tensor(
                out=xn_tm[:, 0:DM], in0=x[:], in1=svb(8, DM), op=ALU.mult))
            kb.do("v", lambda e: e.tensor_tensor(
                out=xn_tm[:, 0:DM], in0=xn_tm[:, 0:DM], in1=svb(9, DM), op=ALU.add),
                inc="ve")
            x0_free[tc % 2] = ("ve", kb.cnt["ve"])
            emit_xn_transposes(tc, permuted=False)

        maybe_stop("A", [xnT[:]])
        if DBG_BAR >= 1:
            kb.barrier()
        def prep_slice(h, s, state):
            hp = h % 2
            if s == 0:
                kb.w("t", "dg", state["wnat_ready"])
                for src, dstst in ((wnat_q, WqTst), (wnat_k, WkTst)):
                    for halfb in range(2):
                        b = halfb
                        acquire("t", b)
                        for i in range(8):
                            dc = halfb * 8 + i
                            kb.do("t", lambda e, d=PSB[:, 1024 * b + i * 128:
                                                       1024 * b + (i + 1) * 128],
                                  s2=src[:, dc * 128:(dc + 1) * 128]:
                                  e.transpose(d, s2, idb[:]),
                                  inc=("pe" if i == 7 else None))
                        kb.w("v", "pe")
                        kb.do("v", lambda e, d=dstst[:, halfb * 1024:(halfb + 1) * 1024],
                              s2=bankbf(b): e.tensor_copy(out=d, in_=s2), inc="ve")
                        release(b, "ve")
            elif s in (1, 2):
                wst, dst, bias_col = ((WqTst, QT[hp], h) if s == 1
                                      else (WkTst, KT[hp], 16 + h))
                kb.wf("t", "ve")
                for tb in range(4):
                    b = 2 + tb % 2
                    acquire("t", b)
                    for dc in range(ND):
                        kb.do("t", lambda e, bb=b, w=wst[:, dc * 128:(dc + 1) * 128],
                              r=xnT_rhs(dc, tb * 512, 512), st_=(dc == 0), sp=(dc == 15):
                              e.matmul(bank32(bb), w, r, start=st_, stop=sp),
                              inc=("pe" if dc == 15 else None))
                    kb.w("v", "pe")
                    kb.do("v", lambda e, bb=b, d=dst[:, tb * 512:(tb + 1) * 512],
                          bcol=qkvb_fm[:, bias_col:bias_col + 1]: e.tensor_scalar(
                              out=d, in0=bank32(bb), scalar1=bcol, scalar2=None,
                              op0=ALU.add), inc="ve")
                    release(b, "ve")
                if s == 2:
                    for halfb in range(2):
                        b = halfb
                        acquire("t", b)
                        for i in range(8):
                            dc = halfb * 8 + i
                            kb.do("t", lambda e, d=PSB[:, 1024 * b + i * 128:
                                                       1024 * b + (i + 1) * 128],
                                  s2=wnat_v[:, dc * 128:(dc + 1) * 128]:
                                  e.transpose(d, s2, idb[:]),
                                  inc=("pe" if i == 7 else None))
                        kb.w("v", "pe")
                        kb.do("v", lambda e, d=WvTst[:, halfb * 1024:(halfb + 1) * 1024],
                              s2=bankbf(b): e.tensor_copy(out=d, in_=s2), inc="ve")
                        release(b, "ve")
                    state["wnat_done_pe"] = kb.cnt["pe"]
            else:
                kb.wf("t", "ve")
                for tb in range(4):
                    b = 2 + tb % 2
                    acquire("t", b)
                    for dc in range(ND):
                        kb.do("t", lambda e, bb=b, w=WvTst[:, dc * 128:(dc + 1) * 128],
                              r=xnT_rhs(dc, tb * 512, 512), st_=(dc == 0), sp=(dc == 15):
                              e.matmul(bank32(bb), w, r, start=st_, stop=sp),
                              inc=("pe" if dc == 15 else None))
                    kb.w("v", "pe")
                    kb.do("v", lambda e, bb=b, d=xn_tm[:, tb * 512:(tb + 1) * 512],
                          bcol=qkvb_fm[:, 32 + h:33 + h]: e.tensor_scalar(
                              out=d, in0=bank32(bb), scalar1=bcol, scalar2=None,
                              op0=ALU.add), inc="ve")
                    release(b, "ve")
                kb.w("t", "ve")
                for halfb in range(2):
                    b = halfb
                    acquire("t", b)
                    for i in range(8):
                        kc = halfb * 8 + i
                        kb.do("t", lambda e, d=PSB[:, 1024 * b + i * 128:
                                                   1024 * b + (i + 1) * 128],
                              s2=xn_tm[:, kc * 128:(kc + 1) * 128]:
                              e.transpose(d, s2, idb[:]),
                              inc=("pe" if i == 7 else None))
                    kb.w("v", "pe")
                    for i in range(8):
                        kc = halfb * 8 + i
                        kb.do("v", lambda e, d=Vh[hp][:, kc * 129:kc * 129 + 128],
                              s2=PSB[:, 1024 * b + i * 128: 1024 * b + (i + 1) * 128]:
                              e.tensor_copy(out=d, in_=s2),
                              inc=("ve" if i == 7 else None))
                    release(b, "ve")

        deq_state = {"nat_chunks_done": 0, "gate_rb_emitted": 0, "deq_free": [("dg", 0),
                     ("dg", 0)], "pk_free": [("ve", 0), ("ve", 0)], "pend": [],
                     "xst_free": [0, 0], "down_xbar_emitted": 0}

        def emit_pack_load(kind, idx, c):
            sem, v = deq_state["pk_free"][c % 2]
            kb.w("g", sem, v)
            if kind == "g":
                half, j = idx // NJ, idx % NJ
                kk, lc = j // 11, j % 11
                base = half * FI + 512 * lc + kk
                src = gup_h[base:base + 512, :].rearrange("(c e) w -> c e w", e=8)[:, 0:8:4, :]
                dst = pkb[c % 2][:, 0:GW]
            else:
                oc, hf = idx // 2, idx % 2
                src = dnp_h[oc * 128:(oc + 1) * 128, hf * DWH:(hf + 1) * DWH]
                dst = pkb[c % 2][:, 0:DWH]
            return kb.dma("g", dst, src, inc="dg")

        def emit_dequant_unit(kind, idx, c, ld):
            W = GW if kind == "g" else DWH
            pk = pkb[c % 2][:, 0:W]
            sem, v = deq_state["deq_free"][c % 2]
            kb.w("v", sem, v)
            kb.w("v", "dg", ld)
            for k in range(4):
                sh = 4 * k - 6
                op0 = ALU.logical_shift_right if sh >= 0 else ALU.logical_shift_left
                kb.do("v", lambda e, k=k, a=abs(sh), o=op0, pk=pk, W=W: e.tensor_scalar(
                    out=ubuf[:, k * W:(k + 1) * W], in0=pk, scalar1=a, scalar2=0x01C001C0,
                    op0=o, op1=ALU.bitwise_and))
                kb.do("v", lambda e, k=k, pk=pk, W=W: e.tensor_scalar(
                    out=sbuf_[:, k * W:(k + 1) * W], in0=pk, scalar1=12 - 4 * k,
                    scalar2=0x80008000 - 2 ** 32, op0=ALU.logical_shift_left,
                    op1=ALU.bitwise_and))
            deq_state["pk_free"][c % 2] = ("ve", kb.cnt["ve"] + 1)
            kb.do("v", lambda e, W=W: e.tensor_tensor(
                out=ubuf[:, 0:4 * W], in0=ubuf[:, 0:4 * W], in1=sbuf_[:, 0:4 * W],
                op=ALU.bitwise_or), inc="ve")
            if kind == "g":
                scl = scl_g[:, idx * 16:(idx + 1) * 16]
                nq = 16
            else:
                oc, hf = idx // 2, idx % 2
                scl = scl_d[:, oc * 44 + hf * 22: oc * 44 + (hf + 1) * 22]
                nq = 22
            sclap = scl.unsqueeze(1).unsqueeze(3).broadcast_to([128, 4, nq, 32])
            dq = deqb[c % 2][:, 0:8 * W]
            kb.do("v", lambda e, W=W, nq=nq, s2=sclap, d=dq: e.tensor_tensor(
                out=d.rearrange("p (k q r) -> p k q r", k=4, q=nq),
                in0=ubuf[:, 0:4 * W].bitcast(BF16).rearrange(
                    "p (k q r) -> p k q r", k=4, q=nq),
                in1=s2, op=ALU.mult), inc="ve")
            mult_done = kb.flush("ve")
            kb.w("g", "ve", mult_done)
            if kind == "g":
                dst = wgu_nat[idx * 128:(idx + 1) * 128, :]
                wr = kb.dma("g", dst, dq, inc="dg")
            else:
                oc, hf = idx // 2, idx % 2
                dst = wdn_nat[oc * 128:(oc + 1) * 128, :].rearrange(
                    "p (k j) -> p k j", k=4)[:, :, hf * 704:(hf + 1) * 704]
                wr = kb.dma("g", dst, dq.rearrange("p (k j) -> p k j", k=4), inc="dg")
            deq_state["deq_free"][c % 2] = ("dg", wr)
            return wr

        deq_counter = [0]

        def dequant_package(units):
            pend = deq_state["pend"]
            for u in units:
                pend.append(u)
            while len(pend) > 1:
                kind, idx = pend.pop(0)
                c = deq_counter[0]
                if "ld%d" % c not in deq_state:
                    deq_state["ld%d" % c] = emit_pack_load(kind, idx, c)
                ld = deq_state.pop("ld%d" % c)
                if pend:
                    k2, i2 = pend[0]
                    deq_state["ld%d" % (c + 1)] = emit_pack_load(k2, i2, c + 1)
                emit_dequant_unit(kind, idx, c, ld)
                deq_counter[0] += 1
                if kind == "g":
                    deq_state["nat_chunks_done"] += 1
                    maybe_emit_gate_xbar()

        def flush_dequant():
            pend = deq_state["pend"]
            while pend:
                kind, idx = pend.pop(0)
                c = deq_counter[0]
                if "ld%d" % c not in deq_state:
                    deq_state["ld%d" % c] = emit_pack_load(kind, idx, c)
                ld = deq_state.pop("ld%d" % c)
                if pend:
                    k2, i2 = pend[0]
                    deq_state["ld%d" % (c + 1)] = emit_pack_load(k2, i2, c + 1)
                emit_dequant_unit(kind, idx, c, ld)
                deq_counter[0] += 1
                if kind == "g":
                    deq_state["nat_chunks_done"] += 1
                    maybe_emit_gate_xbar()

        xbar_rows = [(rb * 2048, min(2048, 2 * FI - rb * 2048)) for rb in range(6)]

        def maybe_emit_gate_xbar():
            while deq_state["gate_rb_emitted"] < 6:
                rb = deq_state["gate_rb_emitted"]
                r0, nr = xbar_rows[rb]
                need = (r0 + nr) // 128
                if deq_state["nat_chunks_done"] < need:
                    return
                njb = nr // 128
                for dc in range(ND):
                    kb.w("y", "dg")
                    kb.w("y", "ds", deq_state["xst_free"][dc % 2])
                    ld = kb.dmat("y", xstage[dc % 2][:, 0:nr],
                                 wgu_nat[r0:r0 + nr, dc * 128:(dc + 1) * 128], inc="ds")
                    kb.w("y", "ds", ld)
                    dst = wgu_t[dc, r0 // 128: r0 // 128 + njb, :, :].transpose([1, 0, 2])
                    wr = kb.dma("y", dst,
                                xstage[dc % 2][:, 0:nr].rearrange("p (j f) -> p j f", f=128),
                                inc="ds")
                    deq_state["xst_free"][dc % 2] = wr
                deq_state["gate_rb_emitted"] += 1

        gate_per_pkg = [3] * 4 + [7] * 4 + [6] * 8
        gate_start = [sum(gate_per_pkg[:i]) for i in range(16)]

        def emit_down_xbar(n):
            while deq_state["down_xbar_emitted"] < min(n, NJ):
                j = deq_state["down_xbar_emitted"]
                kb.w("y", "dg")
                kb.w("y", "ds", deq_state["xst_free"][j % 2])
                ld = kb.dmat("y", xstage[j % 2][:],
                             wdn_nat[0:DM, j * 128:(j + 1) * 128], inc="ds")
                kb.w("y", "ds", ld)
                wr = kb.dma("y", wdn_t[j].transpose([1, 0, 2]),
                            xstage[j % 2][:].rearrange("p (o f) -> p o f", f=512),
                            inc="ds")
                deq_state["xst_free"][j % 2] = wr
                deq_state["down_xbar_emitted"] += 1

        def pkg_units(h):
            us = []
            if h < 4:
                us += [("d", i) for i in range(8 * h, 8 * h + 8)]
            us += [("g", jj) for jj in range(gate_start[h],
                                             gate_start[h] + gate_per_pkg[h])]
            return us

        state = {"wnat_ready": wnat_ready, "wnat_done_pe": 0}
        for s in range(4):
            prep_slice(0, s, state)
        maybe_stop("P0", [xnT[:], QT[0][:], KT[0][:], Vh[0][:]])

        ctxst_free = [0, 0]
        for h in range(H):
            hp = h % 2
            if DBG_BAR >= 2:
                kb.barrier()
            if DBG_SEQPREP and h > 0:
                for s_ in range(4):
                    prep_slice(h, s_, state)
            kb.w("v", "dg", ctxst_free[hp])
            if h + 1 < H:
                kb.w("g", "pe", state["wnat_done_pe"])
                state["wnat_ready"] = wnat_fetch(h + 1)
            for qb in range(4):
                for kc in range(16):
                    b = 4 + kc % 2
                    acquire("t", b)
                    kb.do("t", lambda e, bb=b, l=KT[hp][:, kc * 128:(kc + 1) * 128],
                          r=QT[hp][:, qb * 512:(qb + 1) * 512]:
                          e.matmul(bank32(bb), l, r, start=True, stop=True), inc="pe")
                    kb.w("s", "pe")
                    kb.do("s", lambda e, bb=b, d=A_sb[:, kc * 512:(kc + 1) * 512]:
                          e.activation(out=d, in_=bank32(bb), func=AF.Exp,
                                       scale=cc(CSM), bias=cc(CSH)), inc="ac")
                    release(b, "ac")
                if h + 1 < H and not DBG_SEQPREP:
                    prep_slice(h + 1, qb, state)
                if DBG_BAR >= 3:
                    kb.barrier()
                kb.wf("t", "ac")
                kb.wf("t", "ve")
                for qc in range(4):
                    b = 6 + qc % 2
                    acquire("t", b)
                    for kc in range(16):
                        kb.do("t", lambda e, bb=b,
                              l=A_sb[:, kc * 512 + qc * 128: kc * 512 + (qc + 1) * 128],
                              r=Vh[hp][:, kc * 129:(kc + 1) * 129], st_=(kc == 0),
                              sp=(kc == 15):
                              e.matmul(PSa[:, 512 * bb: 512 * bb + 129], l, r,
                                       start=st_, stop=sp),
                              inc=("pe" if kc == 15 else None))
                    kb.w("v", "pe")
                    kb.do("v", lambda e, bb=b: e.reciprocal(
                        out=sv(10), in_=PSa[:, 512 * bb + 128: 512 * bb + 129]))
                    sp()
                    sp()
                    sp()
                    kb.do("v", lambda e, bb=b, d=ctx_tm[qc % 2][:]: e.tensor_tensor(
                        out=d, in0=PSa[:, 512 * bb: 512 * bb + 128], in1=svb(10, 128),
                        op=ALU.mult), inc="ve")
                    release(b, "ve")
                    b2 = qc % 2
                    acquire("t", b2)
                    kb.wf("t", "ve")
                    kb.do("t", lambda e, bb=b2, s2=ctx_tm[qc % 2][:]:
                          e.transpose(PSB[:, 1024 * bb: 1024 * bb + 128], s2, idb[:]),
                          inc="pe")
                    kb.w("v", "pe")
                    kb.do("v", lambda e, bb=b2,
                          d=ctxTst[hp][:, (qb * 4 + qc) * 128:(qb * 4 + qc + 1) * 128]:
                          e.tensor_copy(out=d, in_=PSB[:, 1024 * bb: 1024 * bb + 128]),
                          inc="ve")
                    release(b2, "ve")
            dequant_package(pkg_units(h))
            if h >= 4:
                emit_down_xbar(4 * (h - 3))
            kb.wf("g", "ve")
            kb.dma("g", ctxT_hbm[h], ctxTst[hp][:], inc="dg")
            ctxst_free[hp] = kb.cnt["dg"]
            maybe_stop("H%d" % h, [ctxTst[hp][:], A_sb[:], QT[hp][:], KT[hp][:],
                                   Vh[hp][:]])
        flush_dequant()
        emit_down_xbar(NJ)


        for sem in ("ve", "ac"):
            kb.flush(sem)
        snapB = dict(kb.cnt)
        for st in ENGS:
            for sem in ("pe", "ve", "ac", "dg"):
                kb.w(st, sem, snapB[sem])
        ctxB.close()

        ctxE = ExitStack()
        esb = lambda name, shape, dt: ctxE.enter_context(nc.sbuf_tensor(name, shape, dt))
        WoT = esb("WoT", [128, ND * DM], BF16)
        woslab_s = esb("wos0", [128, S], BF16)
        woslab = (woslab_s, woslab_s)
        woslab_free = [("ve", 0), ("ve", 0)]
        ctxslab_s = esb("cxs0", [128, 16 * 128], BF16)
        ctxslab = (ctxslab_s, ctxslab_s)
        outbb = esb("outbb", [128, DM], BF16)
        tmpE = esb("tmpE", [128, DM], BF16)
        x1st = esb("x1st", [128, DM], BF16)
        x0e_s = esb("x0e0", [128, DM], BF16)
        x0e = (x0e_s, x0e_s)

        kb.dma("g", outbb[:], bo_h.rearrange("(a b) -> a b", a=1).broadcast_to([128, DM]),
               inc="dg")

        for blk in range(16):
            og, sub = blk // 4, blk % 4
            sem, v = woslab_free[blk % 2]
            kb.w("g", sem, v)
            ld = kb.dma("g", woslab[blk % 2][:],
                        wo_h[og * 512 + sub * 128: og * 512 + (sub + 1) * 128, :], inc="dg")
            kb.w("t", "dg", ld)
            for halfb in range(2):
                b = halfb
                acquire("t", b)
                for i in range(8):
                    dc = halfb * 8 + i
                    kb.do("t", lambda e, d=PSB[:, 1024 * b + i * 128: 1024 * b + (i + 1) * 128],
                          s2=woslab[blk % 2][:, dc * 128:(dc + 1) * 128]:
                          e.transpose(d, s2, idb[:]), inc=("pe" if i == 7 else None))
                kb.w("v", "pe")
                dst = WoT[:].rearrange("p (d o) -> p d o", o=DM)[
                    :, halfb * 8: halfb * 8 + 8, og * 512 + sub * 128: og * 512 + (sub + 1) * 128]
                kb.do("v", lambda e, d=dst, s2=bankbf(b): e.tensor_copy(out=d, in_=s2),
                      inc="ve")
                release(b, "ve")
            woslab_free[0] = woslab_free[1] = ("ve", kb.cnt["ve"])
        wot_ready = kb.flush("ve")

        ctxslab_free = [("pe", 0), ("pe", 0)]
        x0e_free = [("ve", 0), ("ve", 0)]
        x1st_free = [0]
        for tc in range(NT):
            sem, v = ctxslab_free[tc % 2]
            kb.w("y", sem, v)
            src = ctxT_hbm[:, :, tc * 128:(tc + 1) * 128].transpose([1, 0, 2])
            ld = kb.dma("y", ctxslab[tc % 2][:].rearrange("p (h f) -> p h f", f=128), src,
                        inc="ds")
            sem, v = x0e_free[tc % 2]
            kb.w("g", sem, v)
            ldx = kb.dma("g", x0e[tc % 2][:], x_h[tc * 128:(tc + 1) * 128, :], inc="dg")
            kb.w("t", "ds", ld)
            kb.w("t", "ve", wot_ready)
            base_bank = 4 * (tc % 2)
            for og in range(4):
                b = base_bank + og
                acquire("t", b)
                for dc in range(ND):
                    kb.do("t", lambda e, bb=b,
                          l=ctxslab[tc % 2][:, dc * 128:(dc + 1) * 128],
                          r=WoT[:, dc * DM + og * 512: dc * DM + (og + 1) * 512],
                          st_=(dc == 0), sp=(dc == 15):
                          e.matmul(bank32(bb), l, r, start=st_, stop=sp),
                          inc=("pe" if dc == 15 else None))
            ctxslab_free[0] = ctxslab_free[1] = ("pe", kb.cnt["pe"])
            kb.w("v", "pe")
            kb.w("v", "dg")
            kb.w("v", "ac")
            kb.do("v", lambda e, bb=base_bank: e.tensor_tensor(
                out=tmpE[:], in0=PSa[:, 512 * bb: 512 * bb + 2048], in1=outbb[:],
                op=ALU.add), inc="ve")
            for og in range(4):
                release(base_bank + og, "ve")
            ln_stats(tmpE[:])
            ln_chain(CE5)
            kb.w("v", "dg", ldx)
            kb.do("v", lambda e, x=x0e[tc % 2]: e.tensor_tensor(
                out=dump[:], in0=x[:], in1=svb(9, DM), op=ALU.add))
            kb.w("v", "dg", x1st_free[0])
            kb.do("v", lambda e: e.tensor_tensor(
                out=x1st[:], in0=tmpE[:], in1=svb(8, DM), op=ALU.mult))
            kb.do("v", lambda e: e.tensor_tensor(
                out=x1st[:], in0=x1st[:], in1=dump[:], op=ALU.add), inc="ve")
            x0e_free[0] = x0e_free[1] = ("ve", kb.cnt["ve"])
            kb.wf("g", "ve")
            kb.dma("g", x1_hbm[tc * 128:(tc + 1) * 128, :], x1st[:], inc="dg")
            x1st_free[0] = kb.cnt["dg"]

        maybe_stop("OP", [x1_hbm[0:S, :]])
        kb.barrier()
        ctxE.close()
        ctxB0.close()

        ctxC = ExitStack()
        csb = lambda name, shape, dt: ctxC.enter_context(nc.sbuf_tensor(name, shape, dt))
        h_sb = csb("h_sb", [128, NJ * 512], BF16)
        ybuf = csb("ybuf", [128, 4 * DM], BF16)
        wguTb = (csb("wgt0", [128, 16 * 128], BF16), csb("wgt1", [128, 16 * 128], BF16),
                 csb("wgt2", [128, 16 * 128], BF16), csb("wgt3", [128, 16 * 128], BF16))
        wdnTb = (csb("wdt0", [128, 512], BF16), csb("wdt1", [128, 512], BF16))
        sg_sb = (csb("sg0", [128, 512], BF16), csb("sg1", [128, 512], BF16))
        x1b = (csb("x1b0", [128, DM], BF16), csb("x1b1", [128, DM], BF16))
        outst = (csb("os0", [128, DM], F32), csb("os1", [128, DM], F32))

        x1_free = [("ve", 0), ("ve", 0)]
        for tc in range(NT):
            sem, v = x1_free[tc % 2]
            kb.w("y", sem, v)
            ld = kb.dma("y", x1b[tc % 2][:], x1_hbm[tc * 128:(tc + 1) * 128, :], inc="ds")
            kb.w("v", "ds", ld)
            kb.w("s", "ds", ld)
            kb.w("v", "pe")
            ln_stats(x1b[tc % 2][:])
            ln_chain(CE5)
            kb.do("v", lambda e, x=x1b[tc % 2]: e.tensor_tensor(
                out=xn_tm[:, 0:DM], in0=x[:], in1=svb(8, DM), op=ALU.mult))
            kb.do("v", lambda e: e.tensor_tensor(
                out=xn_tm[:, 0:DM], in0=xn_tm[:, 0:DM], in1=svb(9, DM), op=ALU.add),
                inc="ve")
            x1_free[tc % 2] = ("ve", kb.cnt["ve"])
            emit_xn_transposes(tc, permuted=True)

        maybe_stop("LN3", [xnT[:], x1_hbm[0:S, :]])
        wgu_free = [("pe", 0)] * 4
        wdn_free = [("pe", 0), ("pe", 0)]
        sg_free = [0, 0]
        outst_free = [0, 0]
        for g in range(4):
            kb.wf("t", "ve")
            kb.w("t", "dg")
            kb.w("v", "pe")
            kb.w("s", "pe")
            if g == 0:
                kb.w("y", "ds")
            for j in range(NJ):
                for half in range(2):
                    jj = half * NJ + j
                    sl = (2 * j + half) % 4
                    sem, v = wgu_free[sl]
                    kb.w("y", sem, v)
                    src = wgu_t[:, jj, :, :].transpose([1, 0, 2])
                    ld = kb.dma("y", wguTb[sl][:].rearrange("p (d f) -> p d f", f=128),
                                src, inc="ds")
                    b = 2 * (j % 3) + half
                    acquire("t", b)
                    kb.w("t", "ds", ld)
                    for dc in range(ND):
                        kb.do("t", lambda e, bb=b, l=wguTb[sl][:, dc * 128:(dc + 1) * 128],
                              r=xnT_rhs(dc, g * 512, 512), st_=(dc == 0), sp=(dc == 15):
                              e.matmul(bank32(bb), l, r, start=st_, stop=sp),
                              inc=("pe" if dc == 15 else None))
                    wgu_free[sl] = ("pe", kb.cnt["pe"])
                bg = 2 * (j % 3)
                kb.w("s", "pe")
                kb.w("s", "ve", sg_free[j % 2])
                kb.do("s", lambda e, bb=bg, d=sg_sb[j % 2][:]: e.activation(
                    out=d, in_=bank32(bb), func=AF.Silu), inc="ac")
                kb.wf("v", "ac")
                kb.do("v", lambda e, bb=bg, d=h_sb[:, j * 512:(j + 1) * 512],
                      s2=sg_sb[j % 2][:]: e.tensor_tensor(
                          out=d, in0=s2, in1=bank32(bb + 1), op=ALU.mult), inc="ve")
                sg_free[j % 2] = kb.cnt["ve"]
                release(bg, "ac")
                release(bg + 1, "ve")
            if g == 0:
                maybe_stop("G0", [h_sb[:], wgu_t[0:ND, :, :, :], x1_hbm[0:S, :]])
            kb.wf("t", "ve")
            for og in range(4):
                for b in (4, 5, 6, 7):
                    acquire("t", b)
                for j in range(NJ):
                    sl = j % 2
                    sem, v = wdn_free[sl]
                    kb.w("y", sem, v)
                    ld = kb.dma("y", wdnTb[sl][:], wdn_t[j, og], inc="ds")
                    kb.w("t", "ds", ld)
                    for tc4 in range(4):
                        kb.do("t", lambda e, bb=4 + tc4,
                              l=h_sb[:, j * 512 + tc4 * 128: j * 512 + (tc4 + 1) * 128],
                              r=wdnTb[sl][:], st_=(j == 0), sp=(j == NJ - 1):
                              e.matmul(bank32(bb), l, r, start=st_, stop=sp),
                              inc=("pe" if tc4 == 3 else None))
                    wdn_free[sl] = ("pe", kb.cnt["pe"])
                kb.w("v", "pe")
                for tc4 in range(4):
                    kb.do("v", lambda e, bb=4 + tc4,
                          d=ybuf[:, tc4 * DM + og * 512: tc4 * DM + (og + 1) * 512]:
                          e.tensor_copy(out=d, in_=bank32(bb)),
                          inc=("ve" if tc4 == 3 else None))
                for b in (4, 5, 6, 7):
                    release(b, "ve")
            if g == 0:
                maybe_stop("D0", [ybuf[:], h_sb[:], x1_hbm[0:S, :]])
            for tc4 in range(4):
                tc = g * 4 + tc4
                yrow = ybuf[:, tc4 * DM:(tc4 + 1) * DM]
                kb.w("v", "ve")
                ln_stats(yrow)
                ln_chain(CE5, extra_eps_col=CE11)
                sem, v = x1_free[tc % 2]
                kb.w("y", sem, v)
                ldx = kb.dma("y", x1b[tc % 2][:], x1_hbm[tc * 128:(tc + 1) * 128, :],
                             inc="ds")
                kb.w("v", "ds", ldx)
                kb.do("v", lambda e, x=x1b[tc % 2]: e.tensor_tensor(
                    out=dump[:], in0=x[:], in1=svb(9, DM), op=ALU.add))
                kb.w("v", "so", outst_free[tc4 % 2])
                kb.do("v", lambda e, y=yrow, o=outst[tc4 % 2]: e.tensor_tensor(
                    out=o[:], in0=y, in1=svb(8, DM), op=ALU.mult))
                kb.do("v", lambda e, o=outst[tc4 % 2]: e.tensor_tensor(
                    out=o[:], in0=o[:], in1=dump[:], op=ALU.add), inc="ve")
                x1_free[tc % 2] = ("ve", kb.cnt["ve"])
                kb.wf("y", "ve")
                kb.dma("y", out_h[tc * 128:(tc + 1) * 128, :], outst[tc4 % 2][:], inc="so")
                outst_free[tc4 % 2] = kb.cnt["so"]

        kb.w("y", "so", kb.cnt["so"])
        kb.barrier()
        if os.environ.get("KDUMP"):
            for nm, ap in (("x1d", x1_hbm[:]), ("ctxd", ctxT_hbm.rearrange(
                    "h p s -> (h p) s")), ("wgud", wgu_t.rearrange(
                    "a b c d -> (a b c) d")), ("wdnd", wdn_t.rearrange(
                    "a b c d -> (a b c) d"))):
                dbg = nc.dram_tensor("kd_" + nm, list(ap.shape), ap.dtype,
                                     kind="ExternalOutput").ap()
                kb.dma("g", dbg, ap, inc="dg")
            kb.w("g", "dg")

    except StopEmission:
        kb.w("g", "dg")

    with nc.Block() as block:
        @block.tensor
        def _(e):
            for f in kb.ops["t"]:
                f(e)

        @block.vector
        def _(e):
            for f in kb.ops["v"]:
                f(e)

        @block.scalar
        def _(e):
            for f in kb.ops["s"]:
                f(e)

        @block.sync
        def _(e):
            for f in kb.ops["y"]:
                f(e)

        @block.gpsimd
        def _(e):
            for f in kb.ops["g"]:
                f(e)

    pctx.pop_all()
    return nc


def kernel(L0_attn_hidden_states, L0_attn_Wqkv_w, L0_attn_Wqkv_b, L0_attn_out_w,
           L0_attn_out_b, L0_ffn_gate_up_packed, L0_ffn_gate_up_scales,
           L0_ffn_down_packed, L0_ffn_down_scales):
    import ml_dtypes

    nc = build_core_program()

    x = np.ascontiguousarray(np.asarray(L0_attn_hidden_states, dtype=np.float32))
    shared = {
        "wqkv": np.ascontiguousarray(np.asarray(L0_attn_Wqkv_w, dtype=np.float32)),
        "bqkv": np.ascontiguousarray(np.asarray(L0_attn_Wqkv_b, dtype=np.float32)),
        "wo": np.ascontiguousarray(np.asarray(L0_attn_out_w, dtype=np.float32)),
        "bo": np.ascontiguousarray(np.asarray(L0_attn_out_b, dtype=np.float32)),
        "gup": np.ascontiguousarray(np.pad(np.asarray(L0_ffn_gate_up_packed, dtype=np.int32), ((0, 8), (0, 0)))),
        "gus": np.ascontiguousarray(np.asarray(L0_ffn_gate_up_scales, dtype=np.float16)),
        "dnp": np.ascontiguousarray(np.asarray(L0_ffn_down_packed, dtype=np.int32)),
        "dns": np.ascontiguousarray(np.asarray(L0_ffn_down_scales, dtype=np.float16)),
        "idb": np.eye(128, dtype=ml_dtypes.bfloat16),
        "idh": np.eye(128, dtype=np.float16),
        "idf": np.eye(128, dtype=np.float32),
    }
    in_maps = [dict(shared, x=np.ascontiguousarray(x[b])) for b in range(8)]
    res = run_bass_kernel_spmd(nc, in_maps, core_ids=list(range(8)))
    out = np.stack([np.asarray(r["out"], dtype=np.float32) for r in res.results], axis=0)
    return out
